# version 01 (final)
# baseline (speedup 1.0000x reference)
from contextlib import ExitStack

import numpy as np
import ml_dtypes

import concourse.bass as bass
import concourse.mybir as mybir
from concourse.bass_utils import run_bass_kernel_spmd

F32 = mybir.dt.float32
BF16 = mybir.dt.bfloat16
AF = mybir.ActivationFunctionType
ALU = mybir.AluOpType
AX = mybir.AxisListType

NCORES = 8
D = 2048
KT = 16
NOWN = 2048
NPRE = 1024
NTOK = NPRE + NOWN
NCH = NTOK // 128
NCHOWN = 16
CH0OWN = NPRE // 128
HEADS = 32
DS = 64
PROJ = 8224
INTER = 1408
IT = INTER // 128
NEXP = 18
EBLK = 1024
NEG = -1.0e9

DBG = {"phase": 4}


class Ch:
    __slots__ = ("sem", "n", "step", "name")
    def __init__(self, nc, name, step=1):
        self.sem = nc.alloc_semaphore(name)
        self.n = 0
        self.step = step
        self.name = name


class Prog:
    def __init__(self):
        self.cur = None
        self.eng = None
        self.chains = []
        self._snap = None

    def chain(self, nc, name, step=1):
        c = Ch(nc, name, step)
        self.chains.append(c)
        return c

    def snap(self):
        self._snap = [c.n for c in self.chains]

    def restore(self):
        for c, v in zip(self.chains, self._snap):
            c.n = v

    def emit(self, eng_name, builder, mark=None):
        if self.cur == eng_name:
            inst = builder(self.eng)
            if mark is not None:
                inst.then_inc(mark.sem, mark.step)
        if mark is not None:
            mark.n += mark.step
            return mark.n
        return None

    def wait(self, eng_name, ch, val=None):
        if self.cur == eng_name:
            self.eng.wait_ge(ch.sem, ch.n if val is None else val)


P = Prog()


def build_program():
    nc = bass.Bass("TRN2", target_bir_lowering=False, debug=False)

    t_u = nc.dram_tensor("u", [D, NTOK + 3], F32, kind="ExternalInput").ap()
    t_winT = nc.dram_tensor("winT", [D, PROJ], F32, kind="ExternalInput").ap()
    t_woutT = nc.dram_tensor("woutT", [D, D], F32, kind="ExternalInput").ap()
    t_wgT = nc.dram_tensor("wgT", [D, 16], F32, kind="ExternalInput").ap()
    t_w1T = nc.dram_tensor("w1T", [16, D, INTER], F32, kind="ExternalInput").ap()
    t_w3T = nc.dram_tensor("w3T", [16, D, INTER], F32, kind="ExternalInput").ap()
    t_w2T = nc.dram_tensor("w2T", [16, INTER, D], F32, kind="ExternalInput").ap()
    t_ws1T = nc.dram_tensor("ws1T", [D, 2 * INTER], F32, kind="ExternalInput").ap()
    t_ws3T = nc.dram_tensor("ws3T", [D, 2 * INTER], F32, kind="ExternalInput").ap()
    t_ws2T = nc.dram_tensor("ws2T", [2 * INTER, D], F32, kind="ExternalInput").ap()
    t_zb = nc.dram_tensor("zb", [128, 16], F32, kind="ExternalInput").ap()
    t_cw = nc.dram_tensor("cw", [128, 48 * 4], F32, kind="ExternalInput").ap()
    t_cb = nc.dram_tensor("cb", [128, 48], F32, kind="ExternalInput").ap()
    t_dexp = nc.dram_tensor("dexp", [128, 16], F32, kind="ExternalInput").ap()
    t_rmsw = nc.dram_tensor("rmsw", [128, 16], F32, kind="ExternalInput").ap()
    t_gincl = nc.dram_tensor("gincl", [128, 128], F32, kind="ExternalInput").ap()
    t_gsgt = nc.dram_tensor("gsgt", [128, 128], F32, kind="ExternalInput").ap()
    t_identb = nc.dram_tensor("identb", [128, 128], BF16, kind="ExternalInput").ap()
    t_identf = nc.dram_tensor("identf", [128, 128], F32, kind="ExternalInput").ap()
    t_maskp = nc.dram_tensor("maskp", [128, 128], BF16, kind="ExternalInput").ap()
    t_oh32 = nc.dram_tensor("oh32", [32, 32 * 128], F32, kind="ExternalInput").ap()
    t_ohpair = nc.dram_tensor("ohpair", [32, 16 * 128], BF16, kind="ExternalInput").ap()
    t_onesbc = nc.dram_tensor("onesbc", [128, 1], BF16, kind="ExternalInput").ap()
    t_onesfc = nc.dram_tensor("onesfc", [128, 1], F32, kind="ExternalInput").ap()
    t_onesbr = nc.dram_tensor("onesbr", [1, 128], BF16, kind="ExternalInput").ap()
    t_onesfr = nc.dram_tensor("onesfr", [1, 128], F32, kind="ExternalInput").ap()
    t_cpool = nc.dram_tensor("cpool", [128, 4], F32, kind="ExternalInput").ap()

    dbgk = lambda lvl: "ExternalOutput" if DBG["phase"] <= lvl else "Internal"
    d_x = nc.dram_tensor("d_x", [D, NTOK], BF16, kind=dbgk(1)).ap()
    d_b = nc.dram_tensor("d_b", [D, NTOK], BF16, kind=dbgk(1)).ap()
    d_c = nc.dram_tensor("d_c", [D, NOWN], BF16, kind=dbgk(1)).ap()
    d_zs = nc.dram_tensor("d_zs", [D, NOWN], BF16, kind=dbgk(1)).ap()
    d_sent = nc.dram_tensor("d_sent", [NCHOWN * 128, 1024], BF16, kind=dbgk(1)).ap()
    d_g = nc.dram_tensor("d_g", [D, NOWN], BF16, kind=dbgk(2)).ap()
    d_xn = nc.dram_tensor("d_xn", [D, NOWN], BF16, kind="Internal").ap()
    d_comb = nc.dram_tensor("d_comb", [16, NOWN], F32, kind=dbgk(2)).ap()
    d_yn = nc.dram_tensor("d_yn", [D, NOWN], BF16, kind=dbgk(3)).ap() if DBG["phase"] <= 3 else None
    t_out = nc.dram_tensor("out", [D, NOWN], F32, kind="ExternalOutput").ap()

    dC = P.chain(nc, "dC", 16)
    dU = [P.chain(nc, f"dU{i}", 16) for i in range(2)]
    dW = [P.chain(nc, f"dW{i}", 16) for i in range(2)]
    dS = [P.chain(nc, f"dS{i}", 16) for i in range(2)]
    dXN = P.chain(nc, "dXN", 16)
    dTL = [P.chain(nc, f"dTL{i}", 16) for i in range(2)]
    dSE = P.chain(nc, "dSE", 16)
    dCM = [P.chain(nc, f"dCM{i}", 16) for i in range(2)]
    dCB = P.chain(nc, "dCB", 16)
    dW13 = [P.chain(nc, f"dW13{i}", 16) for i in range(2)]
    dW2 = [P.chain(nc, f"dW2{i}", 16) for i in range(2)]
    dXB = P.chain(nc, "dXB", 16)
    dG = [P.chain(nc, f"dG{i}", 16) for i in range(2)]
    dL = [P.chain(nc, f"dL{i}", 16) for i in range(2)]
    dOH = P.chain(nc, "dOH", 16)
    dG3 = [P.chain(nc, f"dG3{i}", 16) for i in range(2)]
    dU3 = P.chain(nc, "dU3", 16)
    dWO = [P.chain(nc, f"dWO{i}", 16) for i in range(2)]
    dO = [P.chain(nc, f"dO{i}", 16) for i in range(2)]
    cT = P.chain(nc, "cT")
    cV = P.chain(nc, "cV")
    cS = P.chain(nc, "cS")
    cP = P.chain(nc, "cP")

    sb = nc.alloc_sbuf_tensor
    zb_s = sb("zb_s", [128, 16], F32)
    cw_s = sb("cw_s", [128, 48 * 4], F32)
    cb_s = sb("cb_s", [128, 48], F32)
    dexp_s = sb("dexp_s", [128, 16], F32)
    rmsw_s = sb("rmsw_s", [128, 16], F32)
    identb_s = sb("identb_s", [128, 128], BF16)
    identf_s = sb("identf_s", [128, 128], F32)
    maskp_s = sb("maskp_s", [128, 128], BF16)
    onesbc_s = sb("onesbc_s", [128, 1], BF16)
    onesfc_s = sb("onesfc_s", [128, 1], F32)
    onesbr_s = sb("onesbr_s", [1, 128], BF16)
    onesfr_s = sb("onesfr_s", [1, 128], F32)
    cpool_s = sb("cpool_s", [128, 4], F32)
    nc.const_aps.register(F32, 1e-5, cpool_s[:, 0:1])
    nc.const_aps.register(F32, 1.0, cpool_s[:, 1:2])
    dummy = sb("dummy_t", [1, 4], F32)
    logits = sb("logits", [128, 16 * 16], F32)
    ncs1 = sb("ncs1", [128, NCHOWN * 32], F32)
    e1 = sb("e1", [128, NCHOWN * 32], F32)
    decay = sb("decay", [128, NCH * 32], F32)
    etcp = sb("etcp", [128, NCH * 16], F32)
    s_run = sb("s_run", [128, 16 * 64], F32)

    consts = [
        (zb_s, t_zb), (cw_s, t_cw), (cb_s, t_cb), (dexp_s, t_dexp),
        (rmsw_s, t_rmsw),
        (identb_s, t_identb), (identf_s, t_identf), (maskp_s, t_maskp),
        (onesbc_s, t_onesbc),
        (onesfc_s, t_onesfc), (onesbr_s, t_onesbr), (onesfr_s, t_onesfr),
        (cpool_s, t_cpool),
    ]

    ENGS = ["sy", "gp", "te", "ve", "sc"]

    def run_block(program):
        P.snap()
        with nc.Block() as blk:
            def mk(name):
                def body(e):
                    P.cur = name
                    P.eng = e
                    P.restore()
                    program()
                return body
            blk.sync(mk("sy"))
            blk.gpsimd(mk("gp"))
            blk.tensor(mk("te"))
            blk.vector(mk("ve"))
            blk.scalar(mk("sc"))
        P.cur = None
        P.eng = None

    def barrier():
        if P.cur == "sy":
            P.eng.sem_inc(dC.sem, dC.step)
        dC.n += dC.step
        P.emit("gp", lambda e: e.tensor_copy(dummy[0:1, 1:2], dummy[0:1, 0:1]), mark=cP)
        P.emit("ve", lambda e: e.tensor_copy(dummy[0:1, 2:3], dummy[0:1, 0:1]), mark=cV)
        P.emit("sc", lambda e: e.copy(dummy[0:1, 3:4], dummy[0:1, 0:1]), mark=cS)
        tags = [(c, c.n) for c in P.chains if c.n > 0]
        for en in ENGS:
            for c, v in tags:
                P.wait(en, c, v)

    st1 = ExitStack()
    sb1 = lambda n, sh, dt: st1.enter_context(nc.sbuf_tensor(n, sh, dt))
    ps1 = lambda n, sh, dt: st1.enter_context(nc.psum_tensor(n, sh, dt))
    gincl_s = sb1("gincl_s", [128, 128], F32)
    gsgt_s = sb1("gsgt_s", [128, 128], F32)
    wg_s = sb1("wg_s", [128, KT * 16], F32)
    xn_own = sb1("xn_own", [128, KT * NOWN], BF16)
    xn_pre = sb1("xn_pre", [128, KT * NPRE], BF16)
    xn_halo = sb1("xn_halo", [128, KT * 3], BF16)
    ub = [sb1(f"ub{i}", [128, KT * 128], F32) for i in range(2)]
    usq = sb1("usq", [128, KT * 128], F32)
    tf = sb1("tf", [128, KT * 128], F32)
    rowbuf = sb1("rowbuf", [1, 6 * 128], F32)
    rstdc = sb1("rstdc", [128, 1], F32)
    wb = [sb1(f"wb{i}", [128, KT * 128], BF16) for i in range(2)]
    pw = [sb1(f"pw{i}", [128, 515], BF16) for i in range(2)]
    cacc = sb1("cacc", [128, 512], BF16)
    stg = [sb1(f"stg{i}", [128, 512], BF16) for i in range(2)]
    a_cm = sb1("a_cm", [32, NTOK], F32)
    atm = sb1("atm", [128, 32], F32)
    etcr = sb1("etcr", [1, 32], F32)
    xtm = [sb1(f"xtm{i}", [128, NOWN], BF16) for i in range(2)]
    btm = [sb1(f"btm{i}", [128, NOWN], BF16) for i in range(2)]
    xd = sb1("xd", [128, NOWN], BF16)

    pj = [ps1(f"pj{i}", [128, 512], F32) for i in range(4)]
    ph = ps1("ph", [128, 512], F32)
    pa = ps1("pa", [128, 512], F32)
    pb6 = ps1("pb6", [128, 512], F32)
    pb7 = ps1("pb7", [128, 512], F32)

    def phase01():
        for s, t in consts + [(gincl_s, t_gincl), (gsgt_s, t_gsgt)]:
            P.emit("sy", lambda e, s=s, t=t: e.dma_start(out=s[:], in_=t), mark=dC)
        P.emit("sy", lambda e: e.dma_start(out=dummy[:], in_=t_cpool[0:1, 0:4]), mark=dC)
        P.emit("sy", lambda e: e.dma_start(
            out=wg_s.ap().rearrange("p (k e) -> p k e", k=KT),
            in_=t_wgT.rearrange("(k p) e -> p k e", p=128)), mark=dC)
        tag_consts = dC.n
        jorder = [64] + list(range(64))

        def wload(j, buf):
            m = 32 if j == 64 else 128
            c0 = 6144 + 2048 if j == 64 else j * 128
            return P.emit("gp", lambda e: e.dma_start(
                out=wb[buf].ap()[:, 0:KT * m].rearrange("p (k m) -> p k m", k=KT),
                in_=t_winT.rearrange("(k p) m -> p k m", p=128)[:, :, c0:c0 + m]),
                mark=dW[buf])

        wtags = {}
        wtags[0] = wload(jorder[0], 0)

        p0_mm_banks = (pb6, pb7, ph, pa)
        utag = [0, 0]
        lastv = {}
        sum_evac = {}

        def p0_block(bi, width, ucol0, dst, dcol0, dstride, tb):
            pbuf = bi % 2
            P.wait("sy", cV, lastv.get(pbuf, 0))
            utag[pbuf] = P.emit("sy", lambda e: e.dma_start(
                out=ub[pbuf].ap()[:, 0:KT * width].rearrange("p (k t) -> p k t", k=KT),
                in_=t_u.rearrange("(k p) t -> p k t", p=128)[:, :, ucol0:ucol0 + width]),
                mark=dU[pbuf])
            P.wait("sc", dU[pbuf], utag[pbuf])
            tsq = P.emit("sc", lambda e: e.activation(
                usq[:, 0:KT * width], ub[pbuf][:, 0:KT * width], AF.Square), mark=cS)
            P.wait("te", dU[pbuf], utag[pbuf])
            P.wait("te", cS, tsq)
            P.wait("te", cV, sum_evac.get("v", 0))
            P.wait("te", cS, sum_evac.get("s", 0))
            for k in range(KT):
                tag_sum = P.emit("te", lambda e, k=k: e.matmul(
                    pb6[0:1, 0:width], onesfc_s[:],
                    ub[pbuf][:, k * width:(k + 1) * width],
                    start=(k == 0), stop=(k == KT - 1)),
                    mark=cT if k == KT - 1 else None)
            for k in range(KT):
                tag_sq = P.emit("te", lambda e, k=k: e.matmul(
                    pb7[0:1, 0:width], onesfc_s[:],
                    usq[:, k * width:(k + 1) * width],
                    start=(k == 0), stop=(k == KT - 1)),
                    mark=cT if k == KT - 1 else None)
            P.wait("ve", cT, tag_sq)
            t1 = P.emit("ve", lambda e: e.tensor_scalar(
                rowbuf[0:1, 0:width], pb6[0:1, 0:width], 1.0 / D, None, ALU.mult), mark=cV)
            t2 = P.emit("ve", lambda e: e.tensor_scalar(
                rowbuf[0:1, 128:128 + width], pb7[0:1, 0:width], 1.0 / D, None, ALU.mult), mark=cV)
            P.wait("ve", cV, t2)
            t3 = P.emit("ve", lambda e: e.tensor_tensor(
                rowbuf[0:1, 384:384 + width], rowbuf[0:1, 0:width], rowbuf[0:1, 0:width],
                ALU.mult), mark=cV)
            P.wait("ve", cV, t3)
            t4 = P.emit("ve", lambda e: e.tensor_tensor(
                rowbuf[0:1, 256:256 + width], rowbuf[0:1, 128:128 + width],
                rowbuf[0:1, 384:384 + width], ALU.subtract), mark=cV)
            P.wait("sc", cV, t4)
            t5 = P.emit("sc", lambda e: e.activation(
                rowbuf[0:1, 640:640 + width], rowbuf[0:1, 256:256 + width], AF.Ln,
                bias=1e-5), mark=cS)
            P.wait("sc", cS, t5)
            t6 = P.emit("sc", lambda e: e.activation(
                rowbuf[0:1, 512:512 + width], rowbuf[0:1, 640:640 + width], AF.Exp,
                scale=-0.5), mark=cS)
            P.wait("te", cV, t4)
            P.wait("te", cS, t6)
            P.wait("te", cV, lastv.get("bc", 0))
            P.emit("te", lambda e: e.matmul(
                ph[:, 0:width], onesfr_s[:], rowbuf[0:1, 0:width],
                start=True, stop=True))
            P.emit("te", lambda e: e.matmul(
                pa[:, 0:width], onesfr_s[:], rowbuf[0:1, 512:512 + width],
                start=True, stop=True), mark=cT)
            tag_bc = cT.n
            P.wait("ve", cT, tag_bc)
            for k in range(KT):
                tk = P.emit("ve", lambda e, k=k: e.tensor_tensor(
                    tf[:, k * width:(k + 1) * width],
                    ub[pbuf][:, k * width:(k + 1) * width], ph[:, 0:width],
                    ALU.subtract), mark=cV)
            tag_tf = cV.n
            P.wait("ve", cV, tag_tf)
            for k in range(KT):
                P.emit("ve", lambda e, k=k: e.tensor_tensor(
                    dst[:, dcol0 + k * dstride: dcol0 + k * dstride + width],
                    tf[:, k * width:(k + 1) * width], pa[:, 0:width], ALU.mult),
                    mark=cV)
            tag_xn = cV.n
            lastv[pbuf] = tag_xn
            lastv["bc"] = tag_xn
            sum_evac["v"] = tag_xn
            sum_evac["s"] = cS.n
            if tb is not None:
                P.wait("te", cV, tag_tf)
                for k in range(KT):
                    P.emit("te", lambda e, k=k: e.matmul(
                        pb6[:, 256:272],
                        tf[:, k * width:(k + 1) * width],
                        wg_s[:, k * 16:(k + 1) * 16],
                        start=(k == 0), stop=(k == KT - 1)),
                        mark=cT if k == KT - 1 else None)
                P.emit("te", lambda e: e.transpose(
                    pb7.ap()[:, 256:257], rowbuf[0:1, 512:512 + width],
                    identf_s[0:1, 0:1]), mark=cT)
                tag_lg = cT.n
                P.wait("sc", cT, tag_lg)
                trc = P.emit("sc", lambda e: e.copy(rstdc[:], pb7[:, 256:257]), mark=cS)
                P.wait("ve", cT, tag_lg)
                P.wait("ve", cS, trc)
                P.emit("ve", lambda e: e.tensor_scalar(
                    logits[:, tb * 16:(tb + 1) * 16], pb6[:, 256:272], rstdc[:],
                    None, ALU.mult), mark=cV)
                sum_evac["v"] = cV.n

        P.wait("sy", dC, tag_consts)
        P.wait("te", dC, tag_consts)
        P.wait("ve", dC, tag_consts)
        P.wait("sc", dC, tag_consts)
        P.wait("gp", dC, tag_consts)
        p0_block(0, 3, 0, xn_halo, 0, 3, None)
        for b in range(NCH):
            t0 = b * 128
            if t0 < NPRE:
                p0_block(b + 1, 128, 3 + t0, xn_pre, t0, NPRE, None)
            else:
                p0_block(b + 1, 128, 3 + t0, xn_own, t0 - NPRE, NOWN,
                         (t0 - NPRE) // 128)

        barrier()

        def xn_rhs(k, nb):
            if nb < NPRE // 512:
                return xn_pre[:, k * NPRE + nb * 512: k * NPRE + nb * 512 + 512]
            nb2 = nb - NPRE // 512
            return xn_own[:, k * NOWN + nb2 * 512: k * NOWN + nb2 * 512 + 512]

        evac_tags = {}
        spill_tags = {}
        stage_free = {}
        pw_free = {}
        halo_done = {}

        for ji, j in enumerate(jorder):
            bufw = ji % 2
            if ji + 1 < len(jorder):
                P.wait("gp", cT, wtags.get(("used", (ji + 1) % 2), 0))
                wtags[ji + 1] = wload(jorder[ji + 1], (ji + 1) % 2)
            P.wait("te", dW[bufw], wtags[ji])

            is_a = (j == 64)
            is_x = (not is_a) and j < 16
            is_b = (not is_a) and 16 <= j < 32
            is_c = (not is_a) and 32 <= j < 48
            is_z = (not is_a) and j >= 48
            m = 32 if is_a else 128
            nblks = (list(range(NTOK // 512)) if (is_x or is_b or is_a)
                     else list(range(NPRE // 512, NTOK // 512)))
            has_conv = is_x or is_b or is_c

            if has_conv:
                P.wait("te", cS, halo_done.get("s", 0))
                for k in range(KT):
                    if is_c:
                        rhs = xn_pre[:, k * NPRE + NPRE - 3: k * NPRE + NPRE]
                    else:
                        rhs = xn_halo[:, k * 3:(k + 1) * 3]
                    halo_tag = P.emit("te", lambda e, k=k, rhs=rhs, m=m: e.matmul(
                        ph[:, 0:3], wb[bufw][:, k * m:k * m + m], rhs,
                        start=(k == 0), stop=(k == KT - 1)),
                        mark=cT if k == KT - 1 else None)

            for nbi, nb in enumerate(nblks):
                pp = pj[nbi % 4]
                P.wait("te", cS, evac_tags.get(nbi % 4, 0))
                for k in range(KT):
                    mmtag = P.emit("te", lambda e, k=k, nb=nb, m=m: e.matmul(
                        pp[0:m, :], wb[bufw][:, k * m:k * m + m], xn_rhs(k, nb),
                        start=(k == 0), stop=(k == KT - 1)),
                        mark=cT if k == KT - 1 else None)
                if nbi == len(nblks) - 1:
                    wtags[("used", bufw)] = mmtag

                if is_a:
                    P.wait("sc", cT, mmtag)
                    ta = P.emit("sc", lambda e: e.activation(
                        usq[0:32, 0:512], pp[0:32, :], AF.Abs), mark=cS)
                    tr = P.emit("sc", lambda e: e.activation(
                        usq[0:32, 512:1024], pp[0:32, :], AF.Relu), mark=cS)
                    P.wait("sc", cS, ta)
                    te_ = P.emit("sc", lambda e: e.activation(
                        usq[0:32, 1024:1536], usq[0:32, 0:512], AF.Exp, scale=-1.0),
                        mark=cS)
                    P.wait("sc", cS, te_)
                    tl = P.emit("sc", lambda e: e.activation(
                        usq[0:32, 1536:2048], usq[0:32, 1024:1536], AF.Ln, bias=1.0),
                        mark=cS)
                    P.wait("ve", cS, tl)
                    tal = P.emit("ve", lambda e, nb=nb: e.scalar_tensor_tensor(
                        a_cm[:, nb * 512:(nb + 1) * 512], usq[0:32, 512:1024], -1.0,
                        usq[0:32, 1536:2048], ALU.mult, ALU.subtract), mark=cV)
                    evac_tags[nbi % 4] = cS.n
                    P.wait("sc", cV, tal)
                    P.emit("sc", lambda e: e.nop(), mark=cS)
                    evac_tags[nbi % 4] = cS.n
                    continue

                if is_z:
                    st = stg[nbi % 2]
                    P.wait("sc", dS[nbi % 2], stage_free.get(nbi % 2, 0))
                    P.wait("sc", cT, mmtag)
                    tsil = P.emit("sc", lambda e: e.activation(
                        st[:], pp[:], AF.Silu, bias=zb_s[:, j - 48:j - 47]), mark=cS)
                    evac_tags[nbi % 4] = tsil
                    P.wait("sy", cS, tsil)
                    sptag = P.emit("sy", lambda e, j=j, nb=nb, st=st: e.dma_start(
                        out=d_zs[(j - 48) * 128:(j - 47) * 128,
                                 (nb - NPRE // 512) * 512:(nb - NPRE // 512 + 1) * 512],
                        in_=st[:]), mark=dS[nbi % 2])
                    stage_free[nbi % 2] = sptag
                    continue

                cur = nbi % 2
                P.wait("sc", cV, pw_free.get(cur, 0))
                P.wait("sc", cT, mmtag)
                tev = P.emit("sc", lambda e, cur=cur: e.copy(
                    pw[cur][:, 3:515], pp[:]), mark=cS)
                evac_tags[nbi % 4] = tev
                if nbi == 0:
                    P.wait("sc", cT, halo_tag)
                    th = P.emit("sc", lambda e, cur=cur: e.copy(
                        pw[cur][:, 0:3], ph[:, 0:3]), mark=cS)
                    halo_done["s"] = th
                else:
                    prv = (nbi - 1) % 2
                    P.wait("ve", cV, conv_done)
                    th = P.emit("ve", lambda e, cur=cur, prv=prv: e.tensor_copy(
                        pw[cur][:, 0:3], pw[prv][:, 512:515]), mark=cV)
                cvi = j
                P.wait("ve", cS, tev)
                if nbi == 0:
                    P.wait("ve", cS, th)
                else:
                    P.wait("ve", cV, th)
                t0_ = P.emit("ve", lambda e, cur=cur, cvi=cvi: e.tensor_scalar(
                    cacc[:], pw[cur][:, 0:512], cw_s[:, cvi * 4:cvi * 4 + 1], None,
                    ALU.mult), mark=cV)
                prev_t = t0_
                for tap in range(1, 4):
                    P.wait("ve", cV, prev_t)
                    prev_t = P.emit("ve", lambda e, cur=cur, cvi=cvi, tap=tap:
                        e.scalar_tensor_tensor(
                            cacc[:], pw[cur][:, tap:tap + 512],
                            cw_s[:, cvi * 4 + tap:cvi * 4 + tap + 1], cacc[:],
                            ALU.mult, ALU.add), mark=cV)
                conv_done = prev_t
                pw_free[cur] = prev_t
                st = stg[nbi % 2]
                P.wait("sc", dS[nbi % 2], stage_free.get(nbi % 2, 0))
                P.wait("sc", cV, conv_done)
                tsil = P.emit("sc", lambda e, st=st, cvi=cvi: e.activation(
                    st[:], cacc[:], AF.Silu, bias=cb_s[:, cvi:cvi + 1]), mark=cS)
                P.wait("sy", cS, tsil)
                if is_x:
                    dst = d_x[j * 128:(j + 1) * 128, nb * 512:(nb + 1) * 512]
                elif is_b:
                    dst = d_b[(j - 16) * 128:(j - 15) * 128, nb * 512:(nb + 1) * 512]
                else:
                    dst = d_c[(j - 32) * 128:(j - 31) * 128,
                              (nb - NPRE // 512) * 512:(nb - NPRE // 512 + 1) * 512]
                sptag = P.emit("sy", lambda e, dst=dst, st=st: e.dma_start(
                    out=dst, in_=st[:]), mark=dS[nbi % 2])
                stage_free[nbi % 2] = sptag

            if is_a:
                alast = cV.n
                a7s = 0
                a7v = 0
                for c in range(NCH):
                    P.wait("te", cV, alast)
                    P.wait("te", cS, evac_tags.get("atm", 0))
                    tt = P.emit("te", lambda e, c=c: e.transpose(
                        pb6.ap()[:, 0:32], a_cm[:, c * 128:(c + 1) * 128],
                        identf_s[0:32, 0:32]), mark=cT)
                    P.wait("sc", cT, tt)
                    tev = P.emit("sc", lambda e: e.copy(atm[:], pb6[:, 0:32]), mark=cS)
                    evac_tags["atm"] = tev
                    P.wait("te", cS, tev)
                    P.wait("te", cS, a7s)
                    P.wait("te", cV, a7v)
                    P.emit("te", lambda e: e.matmul(
                        pb7[:, 0:32], gincl_s[:], atm[:], start=True, stop=True))
                    P.emit("te", lambda e: e.matmul(
                        pb7[:, 64:96], gsgt_s[:], atm[:], start=True, stop=True))
                    P.emit("te", lambda e: e.matmul(
                        pb7[0:1, 128:160], onesfc_s[:], atm[:], start=True, stop=True),
                        mark=cT)
                    t3m = cT.n
                    P.wait("sc", cT, t3m)
                    td = P.emit("sc", lambda e, c=c: e.activation(
                        decay[:, c * 32:(c + 1) * 32], pb7[:, 64:96], AF.Exp), mark=cS)
                    tet = P.emit("sc", lambda e: e.activation(
                        etcr[:], pb7[0:1, 128:160], AF.Exp), mark=cS)
                    if c >= CH0OWN:
                        oc = c - CH0OWN
                        P.wait("ve", cT, t3m)
                        P.emit("ve", lambda e, oc=oc: e.tensor_scalar(
                            ncs1[:, oc * 32:(oc + 1) * 32], pb7[:, 0:32], -1.0, None,
                            ALU.mult), mark=cV)
                        P.emit("sc", lambda e, oc=oc: e.activation(
                            e1[:, oc * 32:(oc + 1) * 32], pb7[:, 0:32], AF.Exp),
                            mark=cS)
                    P.wait("te", cS, tet)
                    P.wait("te", cS, evac_tags.get("etc", 0))
                    P.emit("te", lambda e: e.matmul(
                        pa[0:64, 0:16], onesfr_s[0:1, 0:64],
                        etcr.ap()[0:1, 0:32].rearrange("a (h two) -> a h two", two=2)[:, :, 0:1].squeeze(2),
                        start=True, stop=True))
                    P.emit("te", lambda e: e.matmul(
                        pa[64:128, 0:16], onesfr_s[0:1, 0:64],
                        etcr.ap()[0:1, 0:32].rearrange("a (h two) -> a h two", two=2)[:, :, 1:2].squeeze(2),
                        start=True, stop=True, tile_position=(0, 64)), mark=cT)
                    tpm = cT.n
                    P.wait("sc", cT, tpm)
                    tpe = P.emit("sc", lambda e, c=c: e.copy(
                        etcp[:, c * 16:(c + 1) * 16], pa[:, 0:16]), mark=cS)
                    evac_tags["etc"] = tpe
                    a7s = cS.n
                    a7v = cV.n

        P.wait("sy", cV, cV.n)
        for k in range(KT):
            P.emit("sy", lambda e, k=k: e.dma_start(
                out=d_xn[k * 128:(k + 1) * 128, :],
                in_=xn_own[:, k * NOWN:(k + 1) * NOWN]), mark=dXN)
        barrier()

        z0 = P.emit("ve", lambda e: e.memset(s_run[:], 0.0), mark=cV)
        tl_tags = {}
        scan_prev = z0
        st_evac = {}
        for c in range(NCH):
            cur = c % 2
            P.wait("sy", cT, tl_tags.get(("used", cur), 0))
            P.wait("sy", cV, tl_tags.get(("usedv", cur), 0))
            t1 = P.emit("sy", lambda e, c=c, cur=cur: e.dma_start(
                out=xtm[cur][:], in_=d_x[:, c * 128:(c + 1) * 128], transpose=True),
                mark=dTL[cur])
            t2 = P.emit("sy", lambda e, c=c, cur=cur: e.dma_start(
                out=btm[cur][:], in_=d_b[:, c * 128:(c + 1) * 128], transpose=True),
                mark=dTL[cur])
            P.wait("ve", dTL[cur], t2)
            P.wait("ve", cT, st_evac.get("xd", 0))
            txd = P.emit("ve", lambda e, c=c, cur=cur: e.tensor_tensor(
                xd.ap().rearrange("p (h d) -> p h d", h=32),
                xtm[cur].ap().rearrange("p (h d) -> p h d", h=32),
                decay.ap()[:, c * 32:(c + 1) * 32].unsqueeze(2).broadcast_to([128, 32, 64]),
                ALU.mult), mark=cV)
            tl_tags[("usedv", cur)] = txd
            if c >= CH0OWN:
                P.wait("gp", cV, scan_prev)
                tsp = P.emit("gp", lambda e, c=c: e.dma_start(
                    out=d_sent[(c - CH0OWN) * 128:(c - CH0OWN + 1) * 128, :],
                    in_=s_run[:]), mark=dSE)
                spill_wait = tsp
            else:
                spill_wait = None
            P.wait("te", dTL[cur], t2)
            P.wait("te", cV, txd)
            for j2 in range(16):
                pk = (pb6 if j2 % 2 == 0 else pb7)
                P.wait("te", cV, st_evac.get(j2 % 2, 0))
                h0, h1 = 2 * j2, 2 * j2 + 1
                P.emit("te", lambda e, h0=h0, cur=cur, pk=pk: e.matmul(
                    pk[0:64, 0:64], btm[cur][:, h0 * 64:(h0 + 1) * 64],
                    xd[:, h0 * 64:(h0 + 1) * 64], start=True, stop=True))
                P.emit("te", lambda e, h1=h1, cur=cur, pk=pk: e.matmul(
                    pk[64:128, 0:64], btm[cur][:, h1 * 64:(h1 + 1) * 64],
                    xd[:, h1 * 64:(h1 + 1) * 64], start=True, stop=True,
                    tile_position=(0, 64)), mark=cT)
                tmm = cT.n
                if j2 == 15:
                    tl_tags[("used", cur)] = tmm
                P.wait("ve", cT, tmm)
                P.wait("ve", cV, scan_prev)
                if spill_wait is not None:
                    P.wait("ve", dSE, spill_wait)
                tsc_ = P.emit("ve", lambda e, c=c, j2=j2, pk=pk: e.scalar_tensor_tensor(
                    s_run[:, j2 * 64:(j2 + 1) * 64], s_run[:, j2 * 64:(j2 + 1) * 64],
                    etcp[:, c * 16 + j2:c * 16 + j2 + 1], pk[:, 0:64],
                    ALU.mult, ALU.add), mark=cV)
                st_evac[j2 % 2] = tsc_
            st_evac["xd"] = cT.n
            scan_prev = cV.n
        barrier()

    run_block(phase01)
    st1.close()

    if DBG["phase"] <= 1:
        return nc

    st2 = ExitStack()
    sb2 = lambda n, sh, dt: st2.enter_context(nc.sbuf_tensor(n, sh, dt))
    xn_blk = sb2("xn_blk", [128, KT * EBLK], BF16)
    y_sb = sb2("y_sb", [128, 16 * EBLK], F32)
    h_sb = sb2("h_sb", [128, IT * EBLK], BF16)
    th_sb = sb2("th_sb", [128, EBLK], BF16)
    s1_sb = sb2("s1_sb", [128, EBLK], BF16)
    cb_sb = sb2("cb_sb", [128, EBLK], F32)
    combrow = [sb2(f"combrow{i}", [1, EBLK], F32) for i in range(2)]
    w13 = [sb2(f"w13_{i}", [128, KT * 128], BF16) for i in range(4)]
    w2b = [sb2(f"w2b_{i}", [128, IT * 128], BF16) for i in range(2)]
    gst = [sb2(f"gst{i}", [128, EBLK], BF16) for i in range(2)]
    rwork = sb2("rwork", [128, 16 * 8], F32)
    rrow = sb2("rrow", [128, 8], F32)
    combT = sb2("combT", [16, 128], F32)

    pf = [st2.enter_context(nc.psum_tensor(f"pf{i}", [128, 512], F32)) for i in range(8)]

    def phase2():
        rt = {}
        wtag = {}
        for tb in range(16):
            lg = logits[:, tb * 16:(tb + 1) * 16]
            t1 = P.emit("ve", lambda e, lg=lg: e.reduce_max(
                rrow[:, 0:1], lg, axis=AX.X), mark=cV)
            P.wait("ve", cV, t1)
            t2 = P.emit("ve", lambda e: e.tensor_scalar(
                rrow[:, 1:2], rrow[:, 0:1], -1.0, None, ALU.mult), mark=cV)
            P.wait("sc", cV, t2)
            P.wait("sc", cT, rt.get("u4", 0))
            t3 = P.emit("sc", lambda e, lg=lg: e.activation(
                rwork[:, 0:16], lg, AF.Exp, bias=rrow[:, 1:2]), mark=cS)
            P.wait("ve", cS, t3)
            t4 = P.emit("ve", lambda e: e.reduce_sum(
                rrow[:, 2:3], rwork[:, 0:16], axis=AX.X), mark=cV)
            P.wait("ve", cV, t4)
            t5 = P.emit("ve", lambda e: e.reciprocal(
                rrow[:, 3:4], rrow[:, 2:3]), mark=cV)
            P.wait("ve", cV, t5)
            t6 = P.emit("ve", lambda e: e.tensor_scalar(
                rwork[:, 16:32], rwork[:, 0:16], rrow[:, 3:4], None, ALU.mult), mark=cV)
            P.wait("ve", cV, t6)
            t7 = P.emit("ve", lambda e: e.tensor_reduce(
                rwork.ap()[:, 32:36].unsqueeze(2),
                rwork.ap()[:, 16:32].rearrange("p (a b) -> p a b", a=4),
                AX.X, ALU.max), mark=cV)
            P.wait("ve", cV, t7)
            t8 = P.emit("ve", lambda e: e.reduce_max(
                rrow[:, 4:5], rwork[:, 32:36], axis=AX.X), mark=cV)
            P.wait("ve", cV, t8)
            t9 = P.emit("ve", lambda e: e.tensor_scalar(
                rwork[:, 36:40], rwork[:, 32:36], rrow[:, 4:5], None, ALU.is_lt),
                mark=cV)
            P.wait("ve", cV, t9)
            ta = P.emit("ve", lambda e: e.tensor_tensor(
                rwork[:, 40:44], rwork[:, 32:36], rwork[:, 36:40], ALU.mult),
                mark=cV)
            P.wait("ve", cV, ta)
            tc = P.emit("ve", lambda e: e.reduce_max(
                rrow[:, 5:6], rwork[:, 40:44], axis=AX.X), mark=cV)
            P.wait("ve", cV, tc)
            td = P.emit("ve", lambda e: e.tensor_scalar(
                rwork.ap()[:, 48:64].rearrange("p (a b) -> p a b", a=4),
                rwork.ap()[:, 32:36].unsqueeze(2).broadcast_to([128, 4, 4]),
                rrow[:, 5:6], None, ALU.is_ge), mark=cV)
            P.wait("ve", cV, td)
            tm1 = P.emit("ve", lambda e: e.tensor_tensor(
                rwork[:, 64:80], rwork[:, 16:32], rwork[:, 48:64], ALU.mult),
                mark=cV)
            P.wait("ve", cV, tm1)
            prev = P.emit("ve", lambda e: e.tensor_copy(
                rwork[:, 96:112], rwork[:, 64:80]), mark=cV)
            for r in range(3):
                P.wait("ve", cV, prev)
                k1 = P.emit("ve", lambda e: e.reduce_max(
                    rrow[:, 6:7], rwork[:, 96:112], axis=AX.X), mark=cV)
                P.wait("ve", cV, k1)
                k2 = P.emit("ve", lambda e: e.tensor_scalar(
                    rwork[:, 80:96], rwork[:, 96:112], rrow[:, 6:7], None, ALU.is_lt),
                    mark=cV)
                P.wait("ve", cV, k2)
                prev = P.emit("ve", lambda e: e.tensor_tensor(
                    rwork[:, 96:112], rwork[:, 96:112], rwork[:, 80:96], ALU.mult),
                    mark=cV)
            P.wait("ve", cV, prev)
            u1 = P.emit("ve", lambda e: e.reduce_max(
                rrow[:, 7:8], rwork[:, 96:112], axis=AX.X), mark=cV)
            P.wait("ve", cV, u1)
            u2 = P.emit("ve", lambda e: e.tensor_scalar(
                rwork[:, 80:96], rwork[:, 64:80], rrow[:, 7:8], None, ALU.is_ge),
                mark=cV)
            P.wait("ve", cV, u2)
            u3 = P.emit("ve", lambda e: e.tensor_tensor(
                rwork[:, 0:16], rwork[:, 16:32], rwork[:, 80:96], ALU.mult), mark=cV)
            P.wait("te", cV, u3)
            P.wait("te", cS, rt.get("ct", 0))
            u4 = P.emit("te", lambda e: e.transpose(
                pf[7][0:16, 0:128], rwork[:, 0:16], identf_s[:]),
                mark=cT)
            rt["u4"] = u4
            P.wait("sc", cT, u4)
            P.wait("sc", dCB, rt.get("sp", 0))
            u5 = P.emit("sc", lambda e: e.copy(combT[:], pf[7][0:16, 0:128]), mark=cS)
            rt["ct"] = u5
            P.wait("sy", cS, u5)
            u6 = P.emit("sy", lambda e, tb=tb: e.dma_start(
                out=d_comb[:, tb * 128:(tb + 1) * 128], in_=combT[:]), mark=dCB)
            rt["sp"] = u6
        P.wait("sy", dCB, rt["sp"])
        combflat = d_comb.rearrange("e t -> (e t)").unsqueeze(0)

        def load_comb(e_, blk, par):
            P.wait("sy", cT, wtag.get(("combused", par), 0))
            return P.emit("sy", lambda e: e.dma_start(
                out=combrow[par][:],
                in_=combflat[0:1, e_ * NOWN + blk * EBLK:
                             e_ * NOWN + blk * EBLK + EBLK]), mark=dCM[par])

        NBLK2 = NOWN // EBLK

        def w13_src(e_, w13sel):
            if e_ < 16:
                return (t_w1T[e_] if w13sel == 0 else t_w3T[e_])
            half = e_ - 16
            t = t_ws1T if w13sel == 0 else t_ws3T
            return t[:, half * INTER:(half + 1) * INTER]

        def issue_load(item):
            kind = item[0]
            if kind == "w13":
                _, e_, it_ = item
                par = it_ % 2
                P.wait("gp", cT, wtag.get(("w13used", par), 0))
                P.emit("gp", lambda e: e.dma_start(
                    out=w13[par * 2].ap()[:, 0:KT * 128].rearrange(
                        "p (k m) -> p k m", k=KT),
                    in_=w13_src(e_, 0).rearrange("(k p) m -> p k m", p=128)
                        [:, :, it_ * 128:(it_ + 1) * 128]), mark=dW13[par])
                tg = P.emit("gp", lambda e: e.dma_start(
                    out=w13[par * 2 + 1].ap()[:, 0:KT * 128].rearrange(
                        "p (k m) -> p k m", k=KT),
                    in_=w13_src(e_, 1).rearrange("(k p) m -> p k m", p=128)
                        [:, :, it_ * 128:(it_ + 1) * 128]), mark=dW13[par])
            else:
                _, e_, dt = item
                par = dt % 2
                P.wait("gp", cT, wtag.get(("w2used", par), 0))
                tg = P.emit("gp", lambda e: e.dma_start(
                    out=w2b[par].ap()[:, 0:IT * 128].rearrange(
                        "p (k m) -> p k m", k=IT),
                    in_=(t_w2T[e_] if e_ < 16 else
                         t_ws2T[(e_ - 16) * INTER:(e_ - 15) * INTER])
                        .rearrange("(k p) m -> p k m", p=128)
                        [:, :, dt * 128:(dt + 1) * 128]), mark=dW2[par])
            wtag[("ldtag", item)] = tg
            return tg

        for blk in range(NBLK2):
            P.wait("sy", cT, wtag.get("xnused", 0))
            lxn = P.emit("sy", lambda e, blk=blk: e.dma_start(
                out=xn_blk.ap().rearrange("p (k t) -> p k t", k=KT),
                in_=d_xn.rearrange("(k p) t -> p k t", p=128)
                    [:, :, blk * EBLK:(blk + 1) * EBLK]), mark=dXB)
            P.wait("te", dXB, lxn)
            sched = []
            for e_ in range(NEXP):
                sched += [("w13", e_, it_) for it_ in range(IT)]
                sched += [("w2", e_, dt) for dt in range(KT)]
            nexti = 0
            issue_load(sched[0])
            nexti = 1

            for e_ in range(NEXP):
                routed = e_ < 16
                if routed:
                    cpar = e_ % 2
                    tcr = load_comb(e_, blk, cpar)
                    P.wait("te", dCM[cpar], tcr)
                    for hf in range(2):
                        P.wait("te", cS, wtag.get("cbf", 0))
                        P.emit("te", lambda e, hf=hf, cpar=cpar: e.matmul(
                            pf[6][:, 0:512], onesfr_s[:],
                            combrow[cpar][0:1, hf * 512:hf * 512 + 512],
                            start=True, stop=True), mark=cT)
                        tcb = cT.n
                        wtag[("combused", cpar)] = tcb
                        P.wait("sc", cT, tcb)
                        P.wait("sc", cP, wtag.get("cbused", 0))
                        tcb2 = P.emit("sc", lambda e, hf=hf: e.copy(
                            cb_sb[:, hf * 512:(hf + 1) * 512], pf[6][:, 0:512]),
                            mark=cS)
                        wtag["cbf"] = tcb2
                    cbtag = wtag["cbf"]
                for it_ in range(IT):
                    par = it_ % 2
                    P.wait("te", dW13[par], wtag[("ldtag", ("w13", e_, it_))])
                    if nexti < len(sched):
                        issue_load(sched[nexti]); nexti += 1
                    wbase = par * 2
                    for hf in range(2):
                        p1 = pf[hf]
                        p3 = pf[2 + hf]
                        P.wait("te", cS, wtag.get(("ev1", hf), 0))
                        P.wait("te", cV, wtag.get(("ev3", hf), 0))
                        for k in range(KT):
                            P.emit("te", lambda e, k=k, hf=hf, wbase=wbase, blk=blk: e.matmul(
                                p1[:], w13[wbase][:, k * 128:(k + 1) * 128],
                                xn_blk[:, k * EBLK + hf * 512:
                                       k * EBLK + hf * 512 + 512],
                                start=(k == 0), stop=(k == KT - 1)),
                                mark=cT if k == KT - 1 else None)
                        for k in range(KT):
                            t13 = P.emit("te", lambda e, k=k, hf=hf, wbase=wbase, blk=blk: e.matmul(
                                p3[:], w13[wbase + 1][:, k * 128:(k + 1) * 128],
                                xn_blk[:, k * EBLK + hf * 512:
                                       k * EBLK + hf * 512 + 512],
                                start=(k == 0), stop=(k == KT - 1)),
                                mark=cT if k == KT - 1 else None)
                        if hf == 1:
                            wtag[("w13used", par)] = t13
                        P.wait("sc", cT, t13)
                        P.wait("sc", cV, wtag.get(("s1w", hf), 0))
                        ts1 = P.emit("sc", lambda e, hf=hf: e.activation(
                            s1_sb[:, hf * 512:(hf + 1) * 512], p1[:], AF.Silu),
                            mark=cS)
                        wtag[("ev1", hf)] = ts1
                        P.wait("ve", cS, ts1)
                        if routed:
                            P.wait("ve", cP, wtag.get(("thw", hf), 0))
                            dstt, dcol = th_sb, hf * 512
                        else:
                            hu = wtag.get("hused_all")
                            if hu is not None:
                                P.wait("ve", cT, hu)
                            dstt, dcol = h_sb, it_ * EBLK + hf * 512
                        tt_ = P.emit("ve", lambda e, hf=hf, dstt=dstt, dcol=dcol: e.tensor_tensor(
                            dstt[:, dcol:dcol + 512], s1_sb[:, hf * 512:(hf + 1) * 512],
                            p3[:], ALU.mult), mark=cV)
                        wtag[("ev3", hf)] = tt_
                        wtag[("s1w", hf)] = tt_
                        if routed:
                            P.wait("gp", cV, tt_)
                            P.wait("gp", cS, cbtag)
                            hu = wtag.get("hused_all")
                            if hu is not None:
                                P.wait("gp", cT, hu)
                            tp = P.emit("gp", lambda e, it_=it_, hf=hf: e.tensor_tensor(
                                h_sb[:, it_ * EBLK + hf * 512: it_ * EBLK + hf * 512 + 512],
                                th_sb[:, hf * 512:(hf + 1) * 512],
                                cb_sb[:, hf * 512:(hf + 1) * 512], ALU.mult), mark=cP)
                            wtag[("thw", hf)] = tp
                            wtag[("hready", it_, hf)] = ("cP", tp)
                        else:
                            wtag[("hready", it_, hf)] = ("cV", tt_)
                wtag["cbused"] = cP.n
                for dt in range(KT):
                    par = dt % 2
                    P.wait("te", dW2[par], wtag[("ldtag", ("w2", e_, dt))])
                    if nexti < len(sched):
                        issue_load(sched[nexti]); nexti += 1
                    for hf in range(2):
                        py = pf[4 + hf]
                        P.wait("te", cV, wtag.get(("yev", hf), 0))
                        if dt == 0:
                            for it_ in range(IT):
                                ch, tg = wtag[("hready", it_, hf)]
                                P.wait("te", cP if ch == "cP" else cV, tg)
                        for k in range(IT):
                            ty2 = P.emit("te", lambda e, k=k, par=par, hf=hf: e.matmul(
                                py[:], w2b[par][:, k * 128:(k + 1) * 128],
                                h_sb[:, k * EBLK + hf * 512: k * EBLK + hf * 512 + 512],
                                start=(k == 0), stop=(k == IT - 1)),
                                mark=cT if k == IT - 1 else None)
                        wtag[("w2used", par)] = ty2
                        P.wait("ve", cT, ty2)
                        P.wait("ve", cV, wtag.get(("yprev", dt, hf), 0))
                        if e_ == 0:
                            tye = P.emit("ve", lambda e, dt=dt, hf=hf: e.tensor_copy(
                                y_sb[:, dt * EBLK + hf * 512: dt * EBLK + hf * 512 + 512],
                                py[:]), mark=cV)
                        else:
                            tye = P.emit("ve", lambda e, dt=dt, hf=hf: e.tensor_tensor(
                                y_sb[:, dt * EBLK + hf * 512: dt * EBLK + hf * 512 + 512],
                                y_sb[:, dt * EBLK + hf * 512: dt * EBLK + hf * 512 + 512],
                                py[:], ALU.add), mark=cV)
                        wtag[("yev", hf)] = tye
                        wtag[("yprev", dt, hf)] = tye
                wtag["hused_all"] = cT.n
            wtag["xnused"] = cT.n
            for dt in range(KT):
                gcur = dt % 2
                P.wait("sc", cV, wtag[("yprev", dt, 0)])
                P.wait("sc", cV, wtag[("yprev", dt, 1)])
                P.wait("sc", dG[gcur], wtag.get(("gsp", gcur), 0))
                tg = P.emit("sc", lambda e, dt=dt, gcur=gcur: e.activation(
                    gst[gcur][:], y_sb[:, dt * EBLK:(dt + 1) * EBLK], AF.Silu), mark=cS)
                P.wait("sy", cS, tg)
                tgs = P.emit("sy", lambda e, dt=dt, blk=blk, gcur=gcur: e.dma_start(
                    out=d_g[dt * 128:(dt + 1) * 128, blk * EBLK:(blk + 1) * EBLK],
                    in_=gst[gcur][:]), mark=dG[gcur])
                wtag[("gsp", gcur)] = tgs
        barrier()

    run_block(phase2)
    st2.close()

    if DBG["phase"] <= 2:
        return nc

    st34 = ExitStack()
    yn_acc = st34.enter_context(nc.sbuf_tensor("yn_acc", [128, KT * NOWN], BF16))
    st3 = ExitStack()
    sb3 = lambda n, sh, dt: st3.enter_context(nc.sbuf_tensor(n, sh, dt))
    xtm3 = [sb3(f"xtm3_{i}", [128, NOWN], BF16) for i in range(2)]
    xcm3 = [sb3(f"xcm3_{i}", [128, NOWN], BF16) for i in range(2)]
    bcm3 = [sb3(f"bcm3_{i}", [128, NOWN], BF16) for i in range(2)]
    ccm3 = [sb3(f"ccm3_{i}", [128, NOWN], BF16) for i in range(2)]
    zs3 = [sb3(f"zs3_{i}", [128, NOWN], BF16) for i in range(2)]
    sent3 = [sb3(f"sent3_{i}", [128, 1024], BF16) for i in range(2)]
    oh32_s = sb3("oh32_s", [32, 32 * 128], F32)
    ohpair_s = sb3("ohpair_s", [32, 16 * 128], BF16)
    ncsT = sb3("ncsT", [32, 128], F32)
    e1t = sb3("e1t", [32, 128], BF16)
    csc = sb3("csc", [128, KT * 128], BF16)
    lt_sb = sb3("lt_sb", [128, 128], BF16)
    s2t = sb3("s2t", [128, 128], BF16)
    yg = sb3("yg", [128, KT * 128], F32)
    yg2 = sb3("yg2", [128, KT * 128], BF16)
    ygt = sb3("ygt", [128, 128], F32)
    r3row = sb3("r3row", [1, 3 * 128], F32)

    p3 = [st3.enter_context(nc.psum_tensor(f"p3_{i}", [128, 512], F32)) for i in range(8)]

    def phase3a():
        ld = {}
        use = {}
        c0 = P.emit("sy", lambda e: e.dma_start(out=oh32_s[:], in_=t_oh32), mark=dOH)
        c1 = P.emit("sy", lambda e: e.dma_start(out=ohpair_s[:], in_=t_ohpair), mark=dOH)
        P.wait("te", dOH, c1)
        for oc in range(NCHOWN):
            c = CH0OWN + oc
            cur = oc % 2
            P.wait("sy", cT, use.get(("t", cur), 0))
            P.wait("sy", cV, use.get(("v", cur), 0))
            l1 = P.emit("sy", lambda e, c=c, cur=cur: e.dma_start(
                out=xtm3[cur][:], in_=d_x[:, c * 128:(c + 1) * 128], transpose=True),
                mark=dL[cur])
            l2 = P.emit("sy", lambda e, c=c, cur=cur: e.dma_start(
                out=xcm3[cur].ap().rearrange("p (k t) -> p k t", k=KT),
                in_=d_x.rearrange("(k p) t -> p k t", p=128)[:, :, c * 128:(c + 1) * 128]),
                mark=dL[cur])
            l3 = P.emit("sy", lambda e, c=c, cur=cur: e.dma_start(
                out=bcm3[cur].ap().rearrange("p (k t) -> p k t", k=KT),
                in_=d_b.rearrange("(k p) t -> p k t", p=128)[:, :, c * 128:(c + 1) * 128]),
                mark=dL[cur])
            l4 = P.emit("sy", lambda e, oc=oc, cur=cur: e.dma_start(
                out=ccm3[cur].ap().rearrange("p (k t) -> p k t", k=KT),
                in_=d_c.rearrange("(k p) t -> p k t", p=128)[:, :, oc * 128:(oc + 1) * 128]),
                mark=dL[cur])
            l5 = P.emit("sy", lambda e, oc=oc, cur=cur: e.dma_start(
                out=zs3[cur].ap().rearrange("p (k t) -> p k t", k=KT),
                in_=d_zs.rearrange("(k p) t -> p k t", p=128)[:, :, oc * 128:(oc + 1) * 128]),
                mark=dL[cur])
            l6 = P.emit("sy", lambda e, oc=oc, cur=cur: e.dma_start(
                out=sent3[cur][:], in_=d_sent[oc * 128:(oc + 1) * 128, :]), mark=dL[cur])

            P.wait("te", cS, use.get("ncsT", 0))
            P.wait("te", cV, use.get(("v", 1 - cur), 0))
            tn = P.emit("te", lambda e, oc=oc: e.transpose(
                p3[7][0:32, 0:128], ncs1[:, oc * 32:(oc + 1) * 32],
                identf_s[:]), mark=cT)
            P.wait("sc", cT, tn)
            tn2 = P.emit("sc", lambda e: e.copy(ncsT[:], p3[7][0:32, 0:128]), mark=cS)
            use["ncsT"] = tn2
            P.wait("te", cS, tn2)
            tE = P.emit("te", lambda e, oc=oc: e.transpose(
                p3[7][0:32, 128:256], e1[:, oc * 32:(oc + 1) * 32],
                identf_s[:]), mark=cT)
            P.wait("sc", cT, tE)
            tE2 = P.emit("sc", lambda e: e.copy(e1t[:], p3[7][0:32, 128:256]), mark=cS)
            P.wait("te", cS, tE2)
            P.wait("te", dL[cur], l6)
            for i in range(KT):
                P.wait("te", cV, use.get(("csc", i % 2), 0))
                tb1 = P.emit("te", lambda e, i=i: e.matmul(
                    p3[2 + (i % 2)][:, 0:128],
                    ohpair_s[:, i * 128:(i + 1) * 128], e1t[:],
                    start=True, stop=True), mark=cT)
                P.wait("ve", cT, tb1)
                tb2 = P.emit("ve", lambda e, i=i, cur=cur: e.tensor_tensor(
                    csc[:, i * 128:(i + 1) * 128], ccm3[cur][:, i * 128:(i + 1) * 128],
                    p3[2 + (i % 2)][:, 0:128], ALU.mult), mark=cV)
                use[("csc", i % 2)] = tb2
            csc_tag = cV.n

            P.wait("te", dL[cur], l6)
            P.wait("te", cV, csc_tag)
            ygdone = {}
            for j2 in range(16):
                h0, h1 = 2 * j2, 2 * j2 + 1
                pS = p3[j2 % 2]
                pD = p3[2 + (j2 % 2)]
                pY = p3[4 + (j2 % 2)]
                for hi, h in enumerate((h0, h1)):
                    base = hi * 64
                    tile = h // 2
                    P.wait("te", cV, use.get(("pSfree", j2 % 2), 0))
                    tS = P.emit("te", lambda e, h=h, hi=hi, cur=cur, pS=pS, tile=tile: e.matmul(
                        pS[:, 0:128],
                        bcm3[cur][(h % 2) * 64:(h % 2) * 64 + 64, tile * 128:(tile + 1) * 128],
                        ccm3[cur][(h % 2) * 64:(h % 2) * 64 + 64, tile * 128:(tile + 1) * 128],
                        start=True, stop=True), mark=cT)
                    P.wait("te", cS, use.get(("lt", j2 % 2), 0))
                    if j2 < 2 and hi == 0:
                        P.wait("te", cV, csc_tag)
                    P.emit("te", lambda e, h=h, pD=pD: e.matmul(
                        pD[:, 0:128], oh32_s[:, h * 128:(h + 1) * 128], ncsT[:],
                        start=True, stop=False))
                    tD = P.emit("te", lambda e, pD=pD: e.matmul(
                        pD[:, 0:128], identb_s[:], maskp_s[:],
                        start=False, stop=True), mark=cT)
                    P.wait("sc", cT, tD)
                    P.wait("sc", cV, use.get("s2tprev", 0))
                    tL = P.emit("sc", lambda e, oc=oc, h=h, pD=pD: e.activation(
                        lt_sb[:], pD[:, 0:128], AF.Exp,
                        bias=ncs1[:, oc * 32 + h:oc * 32 + h + 1], scale=-1.0), mark=cS)
                    use[("lt", j2 % 2)] = tL
                    P.wait("ve", cS, tL)
                    P.wait("ve", cT, tS)
                    tS2 = P.emit("ve", lambda e: e.tensor_tensor(
                        s2t[:], lt_sb[:], pS[:, 0:128], ALU.mult), mark=cV)
                    use["s2tprev"] = tS2
                    use[("pSfree", j2 % 2)] = tS2
                    P.wait("te", cV, tS2)
                    if hi == 0:
                        P.wait("te", cV, ygdone.get(j2 % 2, 0))
                    tpos = (0, base)
                    P.emit("te", lambda e, h=h, cur=cur, pY=pY, base=base: e.matmul(
                        pY[base:base + 64, 0:128], xtm3[cur][:, h * 64:(h + 1) * 64],
                        s2t[:], start=True, stop=False,
                        tile_position=(0, base)))
                    tY = P.emit("te", lambda e, h=h, cur=cur, pY=pY, base=base, tile=tile: e.matmul(
                        pY[base:base + 64, 0:128],
                        sent3[cur][(h % 2) * 64:(h % 2) * 64 + 64, (h // 2) * 64:(h // 2) * 64 + 64],
                        csc[(h % 2) * 64:(h % 2) * 64 + 64, tile * 128:(tile + 1) * 128],
                        start=False, stop=True,
                        tile_position=((h % 2) * 64, base)), mark=cT)
                tYp = cT.n
                P.wait("ve", cT, tYp)
                P.wait("ve", dL[cur], l6)
                ty1 = P.emit("ve", lambda e, j2=j2, cur=cur, pY=pY: e.scalar_tensor_tensor(
                    ygt[:], xcm3[cur][:, j2 * 128:(j2 + 1) * 128],
                    dexp_s[:, j2:j2 + 1], pY[:, 0:128], ALU.mult, ALU.add), mark=cV)
                P.wait("ve", cV, ty1)
                ty2_ = P.emit("ve", lambda e, j2=j2, cur=cur: e.tensor_tensor(
                    yg[:, j2 * 128:(j2 + 1) * 128], ygt[:],
                    zs3[cur][:, j2 * 128:(j2 + 1) * 128], ALU.mult), mark=cV)
                ygdone[j2 % 2] = ty2_
                P.wait("sc", cV, ty2_)
                P.emit("sc", lambda e, j2=j2: e.activation(
                    yg2[:, j2 * 128:(j2 + 1) * 128], yg[:, j2 * 128:(j2 + 1) * 128],
                    AF.Square), mark=cS)
            sq_tag = cS.n
            use[("t", cur)] = cT.n
            P.wait("te", cS, sq_tag)
            P.wait("te", cS, use.get("rms_s", 0))
            for i in range(KT):
                tms = P.emit("te", lambda e, i=i: e.matmul(
                    p3[7][0:1, 256:384], onesbc_s[:], yg2[:, i * 128:(i + 1) * 128],
                    start=(i == 0), stop=(i == KT - 1)),
                    mark=cT if i == KT - 1 else None)
            P.wait("sc", cT, tms)
            tr1 = P.emit("sc", lambda e: e.activation(
                r3row[0:1, 0:128], p3[7][0:1, 256:384], AF.Ln, scale=1.0 / D,
                bias=1e-5), mark=cS)
            P.wait("sc", cS, tr1)
            tr2 = P.emit("sc", lambda e: e.activation(
                r3row[0:1, 128:256], r3row[0:1, 0:128], AF.Exp, scale=-0.5), mark=cS)
            use["rms_s"] = tr2
            P.wait("te", cS, tr2)
            trb = P.emit("te", lambda e: e.matmul(
                p3[7][:, 384:512], onesfr_s[:], r3row[0:1, 128:256],
                start=True, stop=True), mark=cT)
            P.wait("ve", cT, trb)
            for i in range(KT):
                P.emit("ve", lambda e, i=i, oc=oc: e.scalar_tensor_tensor(
                    yn_acc[:, i * NOWN + oc * 128: i * NOWN + oc * 128 + 128],
                    yg[:, i * 128:(i + 1) * 128], rmsw_s[:, i:i + 1],
                    p3[7][:, 384:512], ALU.mult, ALU.mult), mark=cV)
            use[("v", cur)] = cV.n
            use["rms_s"] = cS.n
        if DBG["phase"] <= 3:
            tagd = None
            P.wait("sy", cV, cV.n)
            for i in range(KT):
                tagd = P.emit("sy", lambda e, i=i: e.dma_start(
                    out=d_yn[i * 128:(i + 1) * 128, :],
                    in_=yn_acc[:, i * NOWN:(i + 1) * NOWN]), mark=dXN)
            P.wait("sy", dXN, tagd)
        barrier()

    run_block(phase3a)
    st3.close()

    if DBG["phase"] <= 3:
        st34.close()
        return nc

    st4 = ExitStack()
    sb4 = lambda n, sh, dt: st4.enter_context(nc.sbuf_tensor(n, sh, dt))
    wo = [sb4(f"wo{i}", [128, KT * 128], BF16) for i in range(2)]
    gb3 = [sb4(f"gb3_{i}", [128, KT * 512], BF16) for i in range(2)]
    ub3 = sb4("ub3", [128, KT * 512], F32)
    ot = [sb4(f"ot{i}", [128, 512], F32) for i in range(2)]
    t1b = sb4("t1b", [128, 512], F32)

    p4 = [st4.enter_context(nc.psum_tensor(f"p4_{i}", [128, 512], F32)) for i in range(2)]

    def phase3b():
        wt = {}
        for blk in range(4):
            cur = blk % 2
            P.wait("sy", cV, wt.get(("gused", cur), 0))
            lg_ = P.emit("sy", lambda e, blk=blk, cur=cur: e.dma_start(
                out=gb3[cur].ap().rearrange("p (k t) -> p k t", k=KT),
                in_=d_g.rearrange("(k p) t -> p k t", p=128)[:, :, blk * 512:(blk + 1) * 512]),
                mark=dG3[cur])
            P.wait("sy", cV, wt.get("uused", 0))
            lu_ = P.emit("sy", lambda e, blk=blk: e.dma_start(
                out=ub3.ap().rearrange("p (k t) -> p k t", k=KT),
                in_=t_u.rearrange("(k p) t -> p k t", p=128)[:, :, 3 + NPRE + blk * 512: 3 + NPRE + (blk + 1) * 512]),
                mark=dU3)
            for dt in range(KT):
                wcur = dt % 2
                P.wait("gp", cT, wt.get(("wused", wcur), 0))
                wl = P.emit("gp", lambda e, dt=dt, wcur=wcur: e.dma_start(
                    out=wo[wcur].ap().rearrange("p (k m) -> p k m", k=KT),
                    in_=t_woutT.rearrange("(k p) m -> p k m", p=128)[:, :, dt * 128:(dt + 1) * 128]),
                    mark=dWO[wcur])
                P.wait("te", dWO[wcur], wl)
                py = p4[dt % 2]
                P.wait("te", cV, wt.get(("oev", dt % 2), 0))
                for k in range(KT):
                    tw = P.emit("te", lambda e, k=k, blk=blk: e.matmul(
                        py[:], wo[wcur][:, k * 128:(k + 1) * 128],
                        yn_acc[:, k * NOWN + blk * 512: k * NOWN + blk * 512 + 512],
                        start=(k == 0), stop=(k == KT - 1)),
                        mark=cT if k == KT - 1 else None)
                wt[("wused", wcur)] = tw
                P.wait("ve", cT, tw)
                P.wait("ve", dG3[cur], lg_)
                P.wait("ve", dU3, lu_)
                P.wait("ve", dO[dt % 2], wt.get(("ost", dt % 2), 0))
                P.wait("ve", cV, wt.get("t1bfree", 0))
                tv1 = P.emit("ve", lambda e, dt=dt, cur=cur: e.tensor_tensor(
                    t1b[:], gb3[cur][:, dt * 512:(dt + 1) * 512], py[:], ALU.mult),
                    mark=cV)
                P.wait("ve", cV, tv1)
                tv2 = P.emit("ve", lambda e, dt=dt: e.tensor_tensor(
                    ot[dt % 2][:], t1b[:], ub3[:, dt * 512:(dt + 1) * 512], ALU.add),
                    mark=cV)
                wt[("oev", dt % 2)] = tv2
                wt["t1bfree"] = tv2
                if dt == KT - 1:
                    wt[("gused", cur)] = tv2
                    wt["uused"] = tv2
                P.wait("sy", cV, tv2)
                to_ = P.emit("sy", lambda e, dt=dt, blk=blk: e.dma_start(
                    out=t_out[dt * 128:(dt + 1) * 128, blk * 512:(blk + 1) * 512],
                    in_=ot[dt % 2][:]), mark=dO[dt % 2])
                wt[("ost", dt % 2)] = to_
        for en in ENGS:
            P.wait(en, dO[0], dO[0].n)
            P.wait(en, dO[1], dO[1].n)

    run_block(phase3b)
    st4.close()
    st34.close()
    return nc


_CACHE = {}


def _consts_np():
    l = np.arange(128)
    gincl = (l[:, None] <= l[None, :]).astype(np.float32)
    gsgt = (l[:, None] > l[None, :]).astype(np.float32)
    identb = np.eye(128, dtype=ml_dtypes.bfloat16)
    identf = np.eye(128, dtype=np.float32)
    maskp = np.where(l[None, :] < l[:, None], 1e9, 0.0).astype(ml_dtypes.bfloat16)
    oh32 = np.zeros((32, 32 * 128), dtype=np.float32)
    for h in range(32):
        oh32[h, h * 128:(h + 1) * 128] = 1.0
    ohpair = np.zeros((32, 16 * 128), dtype=ml_dtypes.bfloat16)
    for i in range(16):
        ohpair[2 * i, i * 128:i * 128 + 64] = 1.0
        ohpair[2 * i + 1, i * 128 + 64:(i + 1) * 128] = 1.0
    return dict(
        gincl=gincl, gsgt=gsgt, identb=identb, identf=identf, maskp=maskp,
        oh32=oh32, ohpair=ohpair,
        onesbc=np.ones((128, 1), dtype=ml_dtypes.bfloat16),
        onesfc=np.ones((128, 1), dtype=np.float32),
        onesbr=np.ones((1, 128), dtype=ml_dtypes.bfloat16),
        onesfr=np.ones((1, 128), dtype=np.float32),
        cpool=np.broadcast_to(np.array([[1e-5, 1.0, 0.0, 0.0]], dtype=np.float32), (128, 4)).copy(),
    )


def _prep_shared(inputs):
    f = np.float32
    w_in = np.asarray(inputs["L0_attn_W_in"], f)
    d = dict(
        winT=np.ascontiguousarray(w_in.T),
        woutT=np.ascontiguousarray(np.asarray(inputs["L0_attn_W_out"], f).T),
        wgT=np.ascontiguousarray(np.asarray(inputs["L0_ffn_Wg"], f).T),
        w1T=np.ascontiguousarray(np.asarray(inputs["L0_ffn_W1"], f).transpose(0, 2, 1)),
        w3T=np.ascontiguousarray(np.asarray(inputs["L0_ffn_W3"], f).transpose(0, 2, 1)),
        w2T=np.ascontiguousarray(np.asarray(inputs["L0_ffn_W2"], f).transpose(0, 2, 1)),
        ws1T=np.ascontiguousarray(np.asarray(inputs["L0_ffn_Ws1"], f).T),
        ws3T=np.ascontiguousarray(np.asarray(inputs["L0_ffn_Ws3"], f).T),
        ws2T=np.ascontiguousarray(np.asarray(inputs["L0_ffn_Ws2"], f).T),
        zb=np.ascontiguousarray(np.asarray(inputs["L0_attn_z_bias"], f).reshape(16, 128).T),
        cw=np.ascontiguousarray(np.asarray(inputs["L0_attn_conv_w"], f).reshape(48, 128, 4).transpose(1, 0, 2).reshape(128, 192)),
        cb=np.ascontiguousarray(np.asarray(inputs["L0_attn_conv_b"], f).reshape(48, 128).T),
        dexp=np.ascontiguousarray(np.repeat(np.asarray(inputs["L0_attn_D"], f), 64).reshape(16, 128).T),
        rmsw=np.ascontiguousarray(np.asarray(inputs["L0_attn_rms_w"], f).reshape(16, 128).T),
    )
    d.update(_consts_np())
    return d


def _prep_u(u_full, core):
    b, q = core // 4, core % 4
    T0 = q * NOWN
    lo = T0 - NPRE - 3
    ub = np.zeros((D, NTOK + 3), dtype=np.float32)
    src_lo = max(0, lo)
    ub[:, src_lo - lo: NTOK + 3] = np.asarray(u_full[b, src_lo:T0 + NOWN, :],
                                              np.float32).T
    return ub


def kernel(**inputs):
    if "nc" not in _CACHE:
        _CACHE["nc"] = build_program()
    nc = _CACHE["nc"]
    shared = _prep_shared(inputs)
    u_full = inputs["L0_attn_u"]
    in_maps = []
    for core in range(NCORES):
        m = dict(shared)
        m["u"] = _prep_u(u_full, core)
        in_maps.append(m)
    res = run_bass_kernel_spmd(nc, in_maps, core_ids=list(range(NCORES)))
    out = np.empty((2, 8192, D), dtype=np.float32)
    for core in range(NCORES):
        b, q = core // 4, core % 4
        out[b, q * NOWN:(q + 1) * NOWN, :] = np.asarray(res.results[core]["out"]).T
    return out
